# version 24
# baseline (speedup 1.0000x reference)
import math
from contextlib import ExitStack

import numpy as np

import concourse.bass as bass
import concourse.mybir as mybir
from concourse.bass_utils import run_bass_kernel_spmd

F32 = mybir.dt.float32
BF16 = mybir.dt.bfloat16
AF = mybir.ActivationFunctionType
ALU = mybir.AluOpType
AX = mybir.AxisListType

ENGINES = ("sync", "gpsimd", "tensor", "vector", "scalar")

D = 2048
ND = 16
TPC = 1024
NHEAD = 16
HID = 5461
HIDP = 5504
NHC = 43
NEL = float(HID * D)
EPS = 1e-5
SQRT_HD = float(np.sqrt(128).astype(np.float32))
NEG = -1000000000.0
XNT_STRIDE = 1376


class Res:
    __slots__ = ("name",)
    _all = []

    def __init__(self, name):
        self.name = name
        Res._all.append(self)


class DSem:
    __slots__ = ("name", "h")
    _all = []

    def __init__(self, name):
        self.name = name
        self.h = None
        DSem._all.append(self)


class Tracker:
    def __init__(self, mode, marks=None):
        self.mode = mode
        self.marks = marks if marks is not None else set()
        self.idx = {e: 0 for e in ENGINES}
        self.mcount = {e: 0 for e in ENGINES}
        self.depcnt = {}
        self.state = {}
        self.dma_n = {}
        self.water = {e: {} for e in ENGINES}
        self.ops = {e: [] for e in ENGINES}
        self.abs_ops = {e: [] for e in ENGINES}
        self.n_waits = 0

    def st(self, res):
        s = self.state.get(res)
        if s is None:
            s = {"w": None, "r": {}}
            self.state[res] = s
        return s


class KB:

    def __init__(self, esems):
        self.esems = esems
        self.t = None

    def _wait_dep(self, eng, dep):
        t = self.t
        if dep is None:
            return
        kind = dep[0]
        if kind == "e":
            peng, pidx = dep[1], dep[2]
            if peng == eng and False:
                return
            if t.mode == "scan":
                t.marks.add((peng, pidx))
                return
            cnt = t.depcnt.get((peng, pidx))
            assert cnt is not None, f"dep ({peng},{pidx}) not resolved"
            wm = t.water[eng]
            key = ("e", peng)
            if wm.get(key, -1) >= cnt:
                return
            wm[key] = cnt
            h = self.esems[peng]
            t.ops[eng].append(lambda E, h=h, c=cnt: E.wait_ge(h, c))
            t.abs_ops[eng].append(("wait", "e_" + peng, cnt))
            t.n_waits += 1
        else:
            dsem, tgt = dep[1], dep[2]
            if t.mode == "scan":
                assert tgt <= t.dma_n[dsem] * 16, f"partial-cumulative DMA wait on {dsem.name}"
                return
            wm = t.water[eng]
            key = ("d", dsem)
            if wm.get(key, -1) >= tgt:
                return
            wm[key] = tgt
            t.ops[eng].append(lambda E, h=dsem.h, c=tgt: E.wait_ge(h, c))
            t.abs_ops[eng].append(("wait", "d_" + dsem.name, tgt))
            t.n_waits += 1

    def I(self, eng, f, reads=(), writes=(), sreads=()):
        t = self.t
        idx = t.idx[eng]
        for r in reads:
            s = t.st(r)
            w = s["w"]
            if w is not None:
                self._wait_dep(eng, w)
        for r in sreads:
            s = t.st(r)
            w = s["w"]
            if w is not None:
                self._wait_dep(eng, w)
        for r in writes:
            s = t.st(r)
            w = s["w"]
            if w is not None and not (w[0] == "e" and w[1] == eng):
                self._wait_dep(eng, w)
            for rd in s["r"].values():
                if not (rd[0] == "e" and rd[1] == eng):
                    self._wait_dep(eng, rd)
        dep = ("e", eng, idx)
        if t.mode == "emit":
            marked = (eng, idx) in t.marks
            if marked:
                t.mcount[eng] += 1
                h = self.esems[eng]
                c = t.mcount[eng]
                t.ops[eng].append(lambda E, f=f, h=h, c=c: f(E).then_inc(h, 1))
                t.abs_ops[eng].append(("inc", "e_" + eng, 1))
            else:
                t.ops[eng].append(lambda E, f=f: f(E))
                t.abs_ops[eng].append(("instr",))
            t.depcnt[(eng, idx)] = t.mcount[eng]
        t.idx[eng] = idx + 1
        for r in reads:
            t.st(r)["r"][("e", eng)] = dep
        for r in sreads:
            t.st(r)["r"][("e", eng)] = dep
        for r in writes:
            s = t.st(r)
            s["w"] = dep
            s["r"] = {}
        return dep

    def dma(self, eng, dsem, f, reads=(), writes=()):
        t = self.t
        for r in reads:
            w = t.st(r)["w"]
            self._wait_dep(eng, w)
        for r in writes:
            s = t.st(r)
            self._wait_dep(eng, s["w"])
            for rd in s["r"].values():
                self._wait_dep(eng, rd)
        t.dma_n[dsem] = t.dma_n.get(dsem, 0) + 1
        tgt = t.dma_n[dsem] * 16
        dep = ("d", dsem, tgt)
        if t.mode == "emit":
            t.ops[eng].append(lambda E, f=f, h=dsem.h: f(E).then_inc(h, 16))
            t.abs_ops[eng].append(("inc", "d_" + dsem.name, 16))
        t.idx[eng] += 1
        for r in reads:
            t.st(r)["r"][("d", dsem)] = dep
        for r in writes:
            s = t.st(r)
            s["w"] = dep
            s["r"] = {}
        return dep


def deadlock_check(abs_ops):
    pos = {e: 0 for e in ENGINES}
    sems = {}
    progress = True
    while progress:
        progress = False
        for e in ENGINES:
            while pos[e] < len(abs_ops[e]):
                op = abs_ops[e][pos[e]]
                if op[0] == "wait" and sems.get(op[1], 0) < op[2]:
                    break
                if op[0] == "inc":
                    sems[op[1]] = sems.get(op[1], 0) + op[2]
                pos[e] += 1
                progress = True
    stuck = {e: (pos[e], len(abs_ops[e]), abs_ops[e][pos[e]] if pos[e] < len(abs_ops[e]) else None)
             for e in ENGINES if pos[e] < len(abs_ops[e])}
    if stuck:
        raise RuntimeError(f"DEADLOCK: {stuck}")


def q_chunk_table(h):
    out = []
    if h < 8:
        for a in range(4):
            kb = 128 + 256 * a
            out.append((kb, kb, 0))
            out.append((kb + 128, kb, 1))
    else:
        out.append((128, 0, 1))
        for a in range(1, 4):
            out.append((256 * a, 256 * a, 0))
            out.append((256 * a + 128, 256 * a, 1))
        out.append((1024, 1024, 0))
    return out


def build_nc():
    nc = bass.Bass("TRN2", target_bir_lowering=False)

    x_in = nc.dram_tensor("x", [1280, D], F32, kind="ExternalInput").ap()
    wq_d = nc.dram_tensor("wqt", [D, D], F32, kind="ExternalInput").ap()
    wk_d = nc.dram_tensor("wkt", [D, D], F32, kind="ExternalInput").ap()
    wv_d = nc.dram_tensor("wvt", [D, D], F32, kind="ExternalInput").ap()
    bqs_d = nc.dram_tensor("bqs", [128, 48], F32, kind="ExternalInput").ap()
    w1t_d = nc.dram_tensor("w1t", [D, HIDP], F32, kind="ExternalInput").ap()
    w3t_d = nc.dram_tensor("w3t", [D, HIDP], F32, kind="ExternalInput").ap()
    w2t_d = nc.dram_tensor("w2t", [HIDP, D], F32, kind="ExternalInput").ap()
    out_d = nc.dram_tensor("out", [TPC, D], F32, kind="ExternalOutput").ap()
    x2s_d = nc.dram_tensor("x2s", [TPC, D], F32, kind="Internal").ap()

    with ExitStack() as ctx:
        E = ctx.enter_context

        ident = E(nc.sbuf_tensor("ident", [128, 128], BF16))
        mask_lo = E(nc.sbuf_tensor("mask_lo", [128, 256], BF16))
        mask_hi = E(nc.sbuf_tensor("mask_hi", [128, 256], BF16))
        ones_c = E(nc.sbuf_tensor("ones_c", [128, 1], F32))
        ones_r = E(nc.sbuf_tensor("ones_r", [1, 128], F32))
        bqs = E(nc.sbuf_tensor("bqs_sb", [128, 48], F32))
        c128 = E(nc.sbuf_tensor("c128", [128, 1], F32))
        rc128 = E(nc.sbuf_tensor("rc128", [128, 1], F32))
        cnel = E(nc.sbuf_tensor("cnel", [128, 1], F32))
        rnel = E(nc.sbuf_tensor("rnel", [128, 1], F32))
        XNT = E(nc.sbuf_tensor("XNT", [128, ND * XNT_STRIDE], BF16))
        ACC = E(nc.sbuf_tensor("ACC", [128, 8 * D], BF16))
        G = E(nc.sbuf_tensor("G", [128, NHC * 1024], BF16))
        XS = [E(nc.sbuf_tensor(f"XS{i}", [128, D], F32)) for i in range(2)]
        WB = [E(nc.sbuf_tensor(f"WB{i}", [128, 1024], F32)) for i in range(4)]
        S_sb = [E(nc.sbuf_tensor(f"S{i}", [128, 256], F32)) for i in range(4)]
        scr = E(nc.sbuf_tensor("scr", [128, D], BF16))
        scr2 = E(nc.sbuf_tensor("scr2", [128, D], BF16))
        _stat_keys = ["sum0", "sum1", "ss0", "ss1", "m0", "m1", "nm20", "nm21", "ve0", "ve1",
                      "r10", "r11", "rstd0", "rstd1", "nmr0", "nmr1",
                      "smax0", "smax1", "smax2", "smax3",
                      "rsum0", "rsum1", "rsum2", "rsum3", "rsum4", "rsum5", "rsum6", "rsum7",
                      "rrec0", "rrec1", "rrec2", "rrec3", "rrec4", "rrec5", "rrec6", "rrec7",
                      "acc1v", "acc3v", "acc2v", "acc1s", "acc3s", "acc2s",
                      "acc1", "acc3", "acc2", "tmpr", "tmps",
                      "s1b", "s3b", "s2b", "h1s", "h3s", "h2s", "nh1s", "nh3s", "nh2s",
                      "ns1", "ns3", "ns2", "nshift"]
        STATS = E(nc.sbuf_tensor("STATS", [128, len(_stat_keys)], F32))
        stat = {k: STATS[:, i:i + 1] for i, k in enumerate(_stat_keys)}
        tot = E(nc.sbuf_tensor("tot", [1, 3], F32))

        pf = [E(nc.psum_tensor(f"pf{i}", [128, 512], F32)) for i in range(6)]
        pb = [E(nc.psum_tensor(f"pbk{i}", [128, 1024], BF16)) for i in range(2)]

        esems = {}
        for e in ENGINES:
            esems[e] = E(nc.semaphore(f"ev_{e}"))
        dsems = {}
        for name in ["xs0", "xs1", "ws0", "ws1", "w30", "w31", "wn00", "wn01", "wn02",
                     "wn10", "wn11", "wn12", "misc", "spill", "store",
                     "xt", "vt0", "vt1", "vt2",
                     "pw0", "pw1", "pw2", "pw3", "pw4", "pw5", "pw6", "pw7"]:
            ds = DSem(name)
            ds.h = E(nc.semaphore(f"dma_{name}"))
            dsems[name] = ds

        R = {}

        def res(name):
            if name not in R:
                R[name] = Res(name)
            return R[name]

        def g_wt(par, mat):
            return G[:, 6144 * par + 2048 * mat: 6144 * par + 2048 * (mat + 1)]

        def g_qt(r):
            return G[:, 24576 + 3584 * r: 24576 + 3584 * r + 1024]

        def g_kt(r):
            return G[:, 24576 + 3584 * r + 1024: 24576 + 3584 * r + 2304]

        def g_vtok(r):
            return G[:, 24576 + 3584 * r + 2304: 24576 + 3584 * r + 3584]

        def g_vt(par):
            return G[:, 35328 + 1280 * par: 35328 + 1280 * (par + 1)]

        def g_p(par, k):
            return G[:, 37888 + 2048 * par + 256 * k: 37888 + 2048 * par + 256 * (k + 1)]

        def g_pt(par, k):
            return G[:, 41984 + 1024 * par + 256 * k: 41984 + 1024 * par + 256 * (k + 1)]

        def acc_tile(t):
            return ACC[:, 2048 * t: 2048 * (t + 1)]

        q_t1t = [ACC[:, 2048 * s:2048 * (s + 1)] for s in range(2)]
        q_t3t = [ACC[:, 4096 + 2048 * s:4096 + 2048 * (s + 1)] for s in range(2)]
        q_g1 = scr[:, 0:1024]

        def xnt_chunk(d):
            return XNT[:, XNT_STRIDE * d: XNT_STRIDE * d + 1280]

        def t2nat(j):
            return XNT[:, 5504 * j: 5504 * (j + 1)]

        kb = KB(esems)

        ID_R = res("ident"); MLO = res("mlo"); MHI = res("mhi")
        ONESC = res("onesc"); ONESR = res("onesr"); BQS = res("bqs"); BQSC = res("bq_s")
        RC = res("rc128"); C128R = res("c128raw"); RNEL = res("rnel"); CNEL = res("cnel")
        XNT_R = [res(f"xnt{d}") for d in range(ND)]
        ACC_R = [res(f"acc{t}") for t in range(8)]
        QT1T_R = [res(f"qt1t{s}") for s in range(2)]
        QT3T_R = [res(f"qt3t{s}") for s in range(2)]
        QG1_R = res("qg1")
        WBR = [res(f"wb{i}") for i in range(8)]
        G_R = [res(f"g{i}") for i in range(NHC)]
        GWT_R = [[res(f"gwt{p}_{m}") for m in range(3)] for p in range(2)]
        GQT_R = [res(f"gqt{r}") for r in range(3)]
        GKT_R = [res(f"gkt{r}") for r in range(3)]
        GVK_R = [res(f"gvk{r}") for r in range(3)]
        GVT_R = [res(f"gvt{p}") for p in range(2)]
        GP_R = [[res(f"gp{p}_{k}") for k in range(8)] for p in range(2)]
        GPT_R = [[res(f"gpt{p}_{k}") for k in range(4)] for p in range(2)]
        XS_R = [res(f"xs{i}") for i in range(2)]
        SSB_R = [res(f"ssb{i}") for i in range(4)]
        SCR_R = res("scr")
        SCR2_R = res("scr2")
        PF_R = [res(f"pf{i}") for i in range(6)]
        PB_R = [res(f"pbk{i}") for i in range(2)]
        ST_R = {k: res(f"st_{k}") for k in stat}
        TOT_R = res("tot")
        X2S_R = [res(f"x2s{t}") for t in range(8)]

        def walk():
            kb.I("gpsimd", lambda E: E.memset(ident[:], 0.0), writes=[ID_R])
            kb.I("gpsimd", lambda E: E.affine_select(
                out=ident[:], in_=ident[:], compare_op=ALU.not_equal, fill=1.0,
                base=0, pattern=[[-1, 128]], channel_multiplier=1),
                reads=[ID_R], writes=[ID_R])
            kb.I("gpsimd", lambda E: E.memset(mask_lo[:], 0.0), writes=[MLO])
            kb.I("gpsimd", lambda E: E.affine_select(
                out=mask_lo[:], in_=mask_lo[:], compare_op=ALU.is_ge, fill=NEG,
                base=0, pattern=[[-1, 256]], channel_multiplier=1),
                reads=[MLO], writes=[MLO])
            kb.I("gpsimd", lambda E: E.memset(mask_hi[:], 0.0), writes=[MHI])
            kb.I("gpsimd", lambda E: E.affine_select(
                out=mask_hi[:], in_=mask_hi[:], compare_op=ALU.is_ge, fill=NEG,
                base=128, pattern=[[-1, 256]], channel_multiplier=1),
                reads=[MHI], writes=[MHI])
            kb.I("gpsimd", lambda E: E.memset(ones_c[:], 1.0), writes=[ONESC])
            kb.I("gpsimd", lambda E: E.memset(ones_r[:], 1.0), writes=[ONESR])
            for ak in ("acc1v", "acc3v", "acc2v", "acc1s", "acc3s", "acc2s"):
                kb.I("gpsimd", lambda E, ak=ak: E.memset(stat[ak], 0.0), writes=[ST_R[ak]])
            kb.I("vector", lambda E: E.memset(stat["nshift"], -20.0), writes=[ST_R["nshift"]])
            kb.I("vector", lambda E: E.memset(c128[:], SQRT_HD), writes=[C128R])
            kb.I("vector", lambda E: E.reciprocal(rc128[:], c128[:]), reads=[C128R], writes=[RC])
            kb.I("vector", lambda E: E.memset(cnel[:], NEL), writes=[CNEL])
            kb.I("vector", lambda E: E.reciprocal(rnel[:], cnel[:]), reads=[CNEL], writes=[RNEL])
            kb.dma("sync", dsems["misc"], lambda E: E.dma_start(out=bqs[:], in_=bqs_d[:]),
                   writes=[BQS])

            def ln_stats(src_ap, src_res, par, extra_reads=(), trash=None):
                tb, tr = trash if trash is not None else (scr, SCR_R)
                p = str(par)
                kb.I("vector", lambda E, a=src_ap: E.reduce_sum(stat["sum" + p], a, axis=AX.X),
                     reads=[src_res, *extra_reads], writes=[ST_R["sum" + p]])
                kb.I("scalar", lambda E, a=src_ap, tb=tb: E.activation(
                    tb[:], a, AF.Square, accum_out=stat["ss" + p]),
                    reads=[src_res, *extra_reads], writes=[tr, ST_R["ss" + p]])
                kb.I("vector", lambda E: E.tensor_scalar(
                    stat["m" + p], stat["sum" + p], 1.0 / D, None, ALU.mult, ALU.bypass),
                    reads=[ST_R["sum" + p]], writes=[ST_R["m" + p]])
                kb.I("vector", lambda E: E.tensor_scalar(
                    stat["nm2" + p], stat["m" + p], stat["m" + p], -1.0, ALU.mult, ALU.mult),
                    reads=[ST_R["m" + p]], sreads=[ST_R["m" + p]], writes=[ST_R["nm2" + p]])
                kb.I("vector", lambda E: E.scalar_tensor_tensor(
                    stat["ve" + p], stat["ss" + p], 1.0 / D, stat["nm2" + p], ALU.mult, ALU.add),
                    reads=[ST_R["ss" + p], ST_R["nm2" + p]], writes=[ST_R["ve" + p]])
                kb.I("vector", lambda E: E.tensor_scalar(
                    stat["ve" + p], stat["ve" + p], EPS, None, ALU.add, ALU.bypass),
                    reads=[ST_R["ve" + p]], writes=[ST_R["ve" + p]])
                kb.I("vector", lambda E: E.reciprocal(stat["r1" + p], stat["ve" + p]),
                     reads=[ST_R["ve" + p]], writes=[ST_R["r1" + p]])
                kb.I("scalar", lambda E: E.activation(stat["rstd" + p], stat["r1" + p], AF.Sqrt),
                     reads=[ST_R["r1" + p]], writes=[ST_R["rstd" + p]])
                kb.I("vector", lambda E: E.tensor_scalar(
                    stat["nmr" + p], stat["m" + p], stat["rstd" + p], -1.0, ALU.mult, ALU.mult),
                    reads=[ST_R["m" + p]], sreads=[ST_R["rstd" + p]], writes=[ST_R["nmr" + p]])
                return "rstd" + p, "nmr" + p

            def ln_apply(src_ap, src_res, dst_ap, dst_res, rk, nk, dtypeout_bf16=True):
                kb.I("scalar", lambda E, a=src_ap, o=dst_ap: E.activation(
                    o, a, AF.Identity, bias=stat[nk], scale=stat[rk]),
                    reads=[src_res], sreads=[ST_R[rk], ST_R[nk]], writes=[dst_res])

            xr3 = XNT.ap().rearrange("p (d c) -> p d c", c=XNT_STRIDE)
            gr128 = G.ap().rearrange("p (c f) -> p c f", f=128)

            def xpose_tile_to_xnt(src_buf, src_res, col0):
                kb.dma("sync", dsems["xt"],
                       lambda E, s=src_buf, c=col0: E.dma_start(
                           out=xr3[:, :, c:c + 128], in_=s[:], transpose=True),
                       reads=[src_res], writes=list(XNT_R))

            def wb_slot(k):
                k = k % 8
                return WB[k // 2][:, 512 * (k % 2):512 * (k % 2) + 512], WBR[k]

            phaseA = []
            for mat, src_d, acck in ((0, w1t_d, "acc1"), (1, w3t_d, "acc3")):
                for a in range(ND):
                    for o in range(0, HIDP, 512):
                        w = min(512, HIDP - o)
                        phaseA.append((src_d[128 * a:128 * (a + 1), o:o + w], w, acck))
            for i in range(NHC):
                for o in range(0, D, 512):
                    phaseA.append((w2t_d[128 * i:128 * (i + 1), o:o + 512], 512, "acc2"))
            _engs = ["vector"] * 40 + [("scalar", "vector")[i % 2] for i in range(len(phaseA) - 40)]
            phaseA = [(s, w, a, e) for (s, w, a), e in zip(phaseA, _engs)]
            pa_iter = iter(phaseA)
            pa_state = {"k": 0}

            def emit_phaseA(n):
                for _ in range(n):
                    item = next(pa_iter, None)
                    if item is None:
                        return
                    src, w, acck, eng = item
                    acck = acck + ("v" if eng == "vector" else "s")
                    k = pa_state["k"]
                    pa_state["k"] += 1
                    sub, sres = wb_slot(k)
                    tk = "tmpr" if eng == "vector" else "tmps"
                    if k < 40:
                        ieng = "sync" if k % 2 == 0 else "scalar"
                    else:
                        ieng = "gpsimd" if k % 2 == 0 else "sync"
                    kb.dma(ieng, dsems[f"pw{k % 8}"],
                           lambda E, s=src, w=w, sub=sub: E.dma_start(out=sub[:, 0:w], in_=s),
                           writes=[sres])
                    if eng == "vector":
                        kb.I("vector", lambda E, w=w, sub=sub, tk=tk: E.tensor_reduce(
                            stat[tk], sub[:, 0:w], AX.X, ALU.add, apply_absolute_value=True),
                            reads=[sres], writes=[ST_R[tk]])
                        kb.I("vector", lambda E, a=acck, tk=tk: E.tensor_tensor(
                            stat[a], stat[a], stat[tk], ALU.add),
                            reads=[ST_R[acck], ST_R[tk]], writes=[ST_R[acck]])
                    else:
                        kb.I("scalar", lambda E, w=w, sub=sub, tk=tk: E.activation(
                            scr[:, 0:w], sub[:, 0:w], AF.Abs, accum_out=stat[tk]),
                            reads=[sres], writes=[SCR_R, ST_R[tk]])
                        kb.I("scalar", lambda E, a=acck, tk=tk: E.activation(
                            stat[a], stat[a], AF.Identity, bias=stat[tk], scale=1.0),
                            reads=[ST_R[acck]], sreads=[ST_R[tk]], writes=[ST_R[acck]])

            def wload(h):
                par = h % 2
                for mi, wd in enumerate((wq_d, wk_d, wv_d)):
                    b0 = (6144 * par + 2048 * mi) // 128
                    kb.dma("gpsimd", dsems[f"wn{par}{mi}"],
                           lambda E, wd=wd, h=h, b0=b0: E.dma_start(
                               out=gr128[:, b0:b0 + 16, :],
                               in_=wd[:, 128 * h:128 * (h + 1)]
                               .rearrange("(a p) c -> p a c", p=128)),
                           writes=[GWT_R[par][mi]])

            mblocks = [(0, 128, 512, "q", 0), (0, 640, 512, "q", 512),
                       (1, 0, 512, "k", 0), (1, 512, 512, "k", 512), (1, 1024, 256, "k", 1024),
                       (2, 0, 512, "v", 0), (2, 512, 512, "v", 512), (2, 1024, 256, "v", 1024)]

            def m_block(h, bi):
                par = h % 2
                r = h % 3
                mat, c0, w, kind, d0 = mblocks[bi]
                bank = bi % 2
                for d in range(ND):
                    kb.I("tensor", lambda E, par=par, mat=mat, d=d, c0=c0, w=w, b=bank: E.matmul(
                        pf[b][:, 0:w], g_wt(par, mat)[:, 128 * d:128 * (d + 1)],
                        xnt_chunk(d)[:, c0:c0 + w], start=(d == 0), stop=(d == ND - 1)),
                        reads=[GWT_R[par][mat], XNT_R[d]], writes=[PF_R[bank]])
                if kind == "q":
                    dst, dres = g_qt(r), GQT_R[r]
                elif kind == "k":
                    dst, dres = g_kt(r), GKT_R[r]
                else:
                    dst, dres = g_vt(par), GVT_R[par]
                if bi % 2 == 0:
                    kb.I("vector", lambda E, dst=dst, d0=d0, w=w, b=bank, mat=mat, h=h: E.tensor_scalar(
                        dst[:, d0:d0 + w], pf[b][:, 0:w],
                        bqs[:, 16 * mat + h:16 * mat + h + 1], None, ALU.add, ALU.bypass),
                        reads=[PF_R[bank]], sreads=[BQS], writes=[dres])
                else:
                    kb.I("scalar", lambda E, dst=dst, d0=d0, w=w, b=bank, mat=mat, h=h: E.activation(
                        dst[:, d0:d0 + w], pf[b][:, 0:w], AF.Identity,
                        bias=bqs[:, 16 * mat + h:16 * mat + h + 1], scale=1.0),
                        reads=[PF_R[bank], BQS], writes=[dres])

            def v_tpose(h):
                par = h % 2
                r = h % 3
                b0 = (24576 + 3584 * r + 2304) // 128
                kb.dma("sync", dsems[f"vt{r}"],
                       lambda E, par=par, b0=b0: E.dma_start(
                           out=gr128[:, b0:b0 + 10, :], in_=g_vt(par)[:], transpose=True),
                       reads=[GVT_R[par]], writes=[GVK_R[r]])

            def s_half(h, qi):
                par = h % 2
                r = h % 3
                qc0, kb0, msel = q_chunk_table(h)[qi]
                sp = qi % 4
                sbank = 2 + (qi % 2)
                mres = MLO if msel == 0 else MHI
                mask = mask_lo if msel == 0 else mask_hi
                kb.I("tensor", lambda E, r=r, qc0=qc0, kb0=kb0, sb=sbank: E.matmul(
                    pf[sb][:, 0:256], g_qt(r)[:, qc0 - 128:qc0], g_kt(r)[:, kb0:kb0 + 256],
                    start=True, stop=True),
                    reads=[GQT_R[r], GKT_R[r]], writes=[PF_R[sbank]])
                kb.I("vector", lambda E, sp=sp, sb=sbank, m=mask: E.scalar_tensor_tensor(
                    S_sb[sp][:], pf[sb][:, 0:256], rc128[:, 0:1], m[:], ALU.mult, ALU.add),
                    reads=[PF_R[sbank], mres], sreads=[RC], writes=[SSB_R[sp]])
                kb.I("scalar", lambda E, sp=sp, qi=qi, par=par: E.activation(
                    g_p(par, qi)[:], S_sb[sp][:], AF.Exp, bias=stat["nshift"],
                    accum_out=stat[f"rsum{qi}"]),
                    reads=[SSB_R[sp]], sreads=[ST_R["nshift"]],
                    writes=[GP_R[par][qi], ST_R[f"rsum{qi}"]])
                kb.I("vector", lambda E, qi=qi: E.reciprocal(
                    stat[f"rrec{qi}"], stat[f"rsum{qi}"]),
                    reads=[ST_R[f"rsum{qi}"]], writes=[ST_R[f"rrec{qi}"]])

            def ptt(h, qi):
                par = h % 2
                sp = qi % 4
                pbank = qi % 2
                for kc in range(2):
                    kb.I("tensor", lambda E, par=par, qi=qi, kc=kc, b=pbank: E.transpose(
                        pb[b][:, 512 + 128 * kc:512 + 128 * (kc + 1)],
                        g_p(par, qi)[:, 128 * kc:128 * (kc + 1)], ident[:]),
                        reads=[GP_R[par][qi], ID_R], writes=[PB_R[pbank]])
                kb.I("vector", lambda E, par=par, sp=sp, b=pbank: E.tensor_copy(
                    g_pt(par, sp)[:], pb[b][:, 512:768]),
                    reads=[PB_R[pbank]], writes=[GPT_R[par][sp]])

            def o_mm(h, qi):
                par = h % 2
                r = h % 3
                qc0, kb0, msel = q_chunk_table(h)[qi]
                sp = qi % 4
                obank = 4 + (qi % 2)
                for kc in range(2):
                    kb.I("tensor", lambda E, par=par, r=r, sp=sp, kc=kc, ob=obank, kb0=kb0: E.matmul(
                        pf[ob][:, 0:128], g_pt(par, sp)[:, 128 * kc:128 * (kc + 1)],
                        g_vtok(r)[:, kb0 + 128 * kc:kb0 + 128 * (kc + 1)],
                        start=(kc == 0), stop=(kc == 1)),
                        reads=[GPT_R[par][sp], GVK_R[r]], writes=[PF_R[obank]])
                tt = qc0 // 128 - 1
                kb.I("scalar", lambda E, qi=qi, ob=obank, tt=tt, h=h: E.activation(
                    acc_tile(tt)[:, 128 * h:128 * (h + 1)], pf[ob][:, 0:128], AF.Identity,
                    bias=0.0, scale=stat[f"rrec{qi}"]),
                    reads=[PF_R[obank]], sreads=[ST_R[f"rrec{qi}"]], writes=[ACC_R[tt]])

            for k in range(10):
                sl = k % 2
                kb.dma("gpsimd", dsems[f"xs{sl}"],
                       lambda E, k=k, sl=sl: E.dma_start(out=XS[sl][:], in_=x_in[128 * k:128 * (k + 1), :]),
                       writes=[XS_R[sl]])
                par = k % 2
                sc_b, sc_r = (scr, SCR_R) if par == 0 else (scr2, SCR2_R)
                rk, nk = ln_stats(XS[sl][:], XS_R[sl], par, trash=(sc_b, sc_r))
                ln_apply(XS[sl][:], XS_R[sl], sc_b[:], sc_r, rk, nk)
                xpose_tile_to_xnt(sc_b, sc_r, 128 * k)
                if k == 2:
                    wload(0)
                elif k == 4:
                    wload(1)
                emit_phaseA(4)

            for h in range(NHEAD):
                if h >= 2:
                    o_mm(h - 2, 7)
                for bi in range(8):
                    m_block(h, bi)
                    if h >= 1:
                        if bi >= 2:
                            ptt(h - 1, bi - 2)
                        s_half(h - 1, bi)
                        if bi >= 3:
                            o_mm(h - 1, bi - 3)
                    emit_phaseA(4)
                v_tpose(h)
                if h + 2 < NHEAD:
                    wload(h + 2)
                if h >= 1:
                    ptt(h - 1, 6)
                    o_mm(h - 1, 5)
                    ptt(h - 1, 7)
                    o_mm(h - 1, 6)
                emit_phaseA(1)
            o_mm(NHEAD - 2, 7)
            for qi in range(8):
                s_half(NHEAD - 1, qi)
                emit_phaseA(1)
                if qi >= 2:
                    ptt(NHEAD - 1, qi - 2)
                if qi >= 3:
                    o_mm(NHEAD - 1, qi - 3)
            ptt(NHEAD - 1, 6)
            o_mm(NHEAD - 1, 5)
            ptt(NHEAD - 1, 7)
            o_mm(NHEAD - 1, 6)
            o_mm(NHEAD - 1, 7)

            emit_phaseA(10000)

            for a in ("acc1", "acc3", "acc2"):
                kb.I("vector", lambda E, a=a: E.tensor_tensor(
                    stat[a], stat[a + "v"], stat[a + "s"], ALU.add),
                    reads=[ST_R[a + "v"], ST_R[a + "s"]], writes=[ST_R[a]])

            for ci, (acck, sk, hk, nhk, nsk) in enumerate((
                    ("acc1", "s1b", "h1s", "nh1s", "ns1"),
                    ("acc3", "s3b", "h3s", "nh3s", "ns3"),
                    ("acc2", "s2b", "h2s", "nh2s", "ns2"))):
                kb.I("tensor", lambda E, a=acck, ci=ci: E.matmul(
                    pf[0][0:1, ci:ci + 1], stat[a], ones_c[:], start=True, stop=True),
                    reads=[ST_R[acck], ONESC], writes=[PF_R[0]])
            kb.I("vector", lambda E: E.tensor_copy(tot[:, 0:3], pf[0][0:1, 0:3]),
                 reads=[PF_R[0]], writes=[TOT_R])
            for ci, (acck, sk, hk, nhk, nsk) in enumerate((
                    ("acc1", "s1b", "h1s", "nh1s", "ns1"),
                    ("acc3", "s3b", "h3s", "nh3s", "ns3"),
                    ("acc2", "s2b", "h2s", "nh2s", "ns2"))):
                kb.I("tensor", lambda E, ci=ci: E.matmul(
                    pf[1][:, ci:ci + 1], ones_r[:], tot[0:1, ci:ci + 1], start=True, stop=True),
                    reads=[TOT_R, ONESR], writes=[PF_R[1]])
            for ci, (acck, sk, hk, nhk, nsk) in enumerate((
                    ("acc1", "s1b", "h1s", "nh1s", "ns1"),
                    ("acc3", "s3b", "h3s", "nh3s", "ns3"),
                    ("acc2", "s2b", "h2s", "nh2s", "ns2"))):
                kb.I("vector", lambda E, ci=ci, sk=sk: E.tensor_scalar(
                    stat[sk], pf[1][:, ci:ci + 1], rnel[:, 0:1], None, ALU.mult, ALU.bypass),
                    reads=[PF_R[1]], sreads=[RNEL], writes=[ST_R[sk]])
                kb.I("vector", lambda E, sk=sk, hk=hk: E.tensor_scalar(
                    stat[hk], stat[sk], 0.5, None, ALU.mult, ALU.bypass),
                    reads=[ST_R[sk]], writes=[ST_R[hk]])
                kb.I("vector", lambda E, sk=sk, nhk=nhk: E.tensor_scalar(
                    stat[nhk], stat[sk], -0.5, None, ALU.mult, ALU.bypass),
                    reads=[ST_R[sk]], writes=[ST_R[nhk]])
                kb.I("vector", lambda E, sk=sk, nsk=nsk: E.tensor_scalar(
                    stat[nsk], stat[sk], -1.0, None, ALU.mult, ALU.bypass),
                    reads=[ST_R[sk]], writes=[ST_R[nsk]])

            wb3 = [WB[j].ap().rearrange("p (a c) -> p a c", c=128) for j in range(4)]

            def p4_load(i):
                for h in range(2):
                    kb.dma("sync", dsems[f"ws{h}"],
                           lambda E, i=i, h=h: E.dma_start(
                               out=wb3[h][:],
                               in_=w1t_d[1024 * h:1024 * (h + 1), 128 * i:128 * (i + 1)]
                               .rearrange("(a p) c -> p a c", p=128)),
                           writes=[WBR[2 * h], WBR[2 * h + 1]])
                    kb.dma("gpsimd", dsems[f"w3{h}"],
                           lambda E, i=i, h=h: E.dma_start(
                               out=wb3[2 + h][:],
                               in_=w3t_d[1024 * h:1024 * (h + 1), 128 * i:128 * (i + 1)]
                               .rearrange("(a p) c -> p a c", p=128)),
                           writes=[WBR[4 + 2 * h], WBR[4 + 2 * h + 1]])

            def p4_quant(i):
                sl = i % 2
                for h in range(2):
                    d1 = q_t1t[sl][:, 1024 * h:1024 * (h + 1)]
                    kb.I("vector", lambda E, h=h, d1=d1: E.tensor_scalar(
                        d1, WB[h][:], stat["h1s"], None, ALU.is_ge, ALU.bypass),
                        reads=[WBR[2 * h], WBR[2 * h + 1]], sreads=[ST_R["h1s"]],
                        writes=[QT1T_R[sl], ACC_R[sl]])
                    kb.I("vector", lambda E, h=h, d1=d1: E.scalar_tensor_tensor(
                        d1, WB[h][:], stat["nh1s"], d1, ALU.is_le, ALU.subtract),
                        reads=[WBR[2 * h], WBR[2 * h + 1], QT1T_R[sl]], sreads=[ST_R["nh1s"]],
                        writes=[QT1T_R[sl], ACC_R[sl]])
                    d3 = q_t3t[sl][:, 1024 * h:1024 * (h + 1)]
                    kb.I("vector", lambda E, h=h, d3=d3: E.tensor_scalar(
                        d3, WB[2 + h][:], stat["h3s"], None, ALU.is_ge, ALU.bypass),
                        reads=[WBR[4 + 2 * h], WBR[4 + 2 * h + 1]], sreads=[ST_R["h3s"]],
                        writes=[QT3T_R[sl], ACC_R[2 + sl]])
                    kb.I("vector", lambda E, h=h, d3=d3: E.scalar_tensor_tensor(
                        d3, WB[2 + h][:], stat["nh3s"], d3, ALU.is_le, ALU.subtract),
                        reads=[WBR[4 + 2 * h], WBR[4 + 2 * h + 1], QT3T_R[sl]], sreads=[ST_R["nh3s"]],
                        writes=[QT3T_R[sl], ACC_R[2 + sl]])

            def p4_mms(i):
                sl = i % 2
                for d in range(ND):
                    for th in range(2):
                        kb.I("tensor", lambda E, sl=sl, d=d, th=th: E.matmul(
                            pf[th][:, 0:512], q_t1t[sl][:, 128 * d:128 * (d + 1)],
                            XNT[:, XNT_STRIDE * d + 512 * th: XNT_STRIDE * d + 512 * (th + 1)],
                            start=(d == 0), stop=(d == ND - 1)),
                            reads=[QT1T_R[sl], XNT_R[d]], writes=[PF_R[th]])
                for d in range(ND):
                    for th in range(2):
                        kb.I("tensor", lambda E, sl=sl, d=d, th=th: E.matmul(
                            pf[2 + th][:, 0:512], q_t3t[sl][:, 128 * d:128 * (d + 1)],
                            XNT[:, XNT_STRIDE * d + 512 * th: XNT_STRIDE * d + 512 * (th + 1)],
                            start=(d == 0), stop=(d == ND - 1)),
                            reads=[QT3T_R[sl], XNT_R[d]], writes=[PF_R[2 + th]])

            def p4_post(i):
                for th in range(2):
                    kb.I("scalar", lambda E, th=th: E.activation(
                        q_g1[:, 512 * th:512 * (th + 1)], pf[th][:, 0:512], AF.Silu,
                        bias=0.0, scale=stat["ns1"]),
                        reads=[PF_R[th]], sreads=[ST_R["ns1"]], writes=[QG1_R, SCR_R])
                for th in range(2):
                    kb.I("vector", lambda E, i=i, th=th: E.scalar_tensor_tensor(
                        G[:, 1024 * i + 512 * th:1024 * i + 512 * (th + 1)],
                        pf[2 + th][:, 0:512], stat["ns3"],
                        q_g1[:, 512 * th:512 * (th + 1)], ALU.mult, ALU.mult),
                        reads=[PF_R[2 + th], QG1_R], sreads=[ST_R["ns3"]], writes=[G_R[i]])

            for t in range(8):
                sl = t % 2
                par = t % 2
                kb.dma("sync" if t % 2 else "gpsimd", dsems[f"xs{sl}"],
                       lambda E, t=t, sl=sl: E.dma_start(
                           out=XS[sl][:], in_=x_in[128 * (t + 1):128 * (t + 2), :]),
                       writes=[XS_R[sl]])
                sc_b, sc_r = (scr, SCR_R) if par == 0 else (scr2, SCR2_R)
                rk, nk = ln_stats(acc_tile(t)[:], ACC_R[t], par, trash=(sc_b, sc_r))
                ln_apply(acc_tile(t)[:], ACC_R[t], sc_b[:], sc_r, rk, nk)
                kb.I("vector", lambda E, sl=sl, sc_b=sc_b: E.tensor_tensor(
                    XS[sl][:], XS[sl][:], sc_b[:], ALU.add),
                    reads=[XS_R[sl], sc_r], writes=[XS_R[sl]])
                rk3, nk3 = ln_stats(XS[sl][:], XS_R[sl], par, trash=(sc_b, sc_r))
                ln_apply(XS[sl][:], XS_R[sl], sc_b[:], sc_r, rk3, nk3)
                xpose_tile_to_xnt(sc_b, sc_r, 128 * t)
                kb.dma("gpsimd", dsems["spill"],
                       lambda E, t=t, sl=sl: E.dma_start(
                           out=x2s_d[128 * t:128 * (t + 1), :], in_=XS[sl][:]),
                       reads=[XS_R[sl]], writes=[X2S_R[t]])
                if t == 0:
                    p4_load(0)
                elif t == 3:
                    p4_quant(0)
                elif t == 5:
                    p4_load(1)

            p4_quant(1)
            p4_load(2)

            for i in range(NHC):
                p4_mms(i)
                p4_post(i)
                if i + 2 < NHC:
                    p4_quant(i + 2)
                if i + 3 < NHC:
                    p4_load(i + 3)

            T2A_R = [res(f"t2a{i}") for i in range(NHC)]

            def xnt_span_res(lo, hi):
                return [XNT_R[c] for c in range(lo // XNT_STRIDE, min(ND - 1, (hi - 1) // XNT_STRIDE) + 1)]

            def t2_quant(dg, i):
                sub, sres = wb_slot(i)
                dst = XNT[:, 512 * i:512 * (i + 1)]
                wres = [T2A_R[i]] + xnt_span_res(512 * i, 512 * (i + 1))
                kb.dma("gpsimd", dsems[f"pw{i % 8}"],
                       lambda E, i=i, dg=dg, sub=sub: E.dma_start(
                           out=sub, in_=w2t_d[128 * i:128 * (i + 1), 512 * dg:512 * (dg + 1)]),
                       writes=[sres])
                kb.I("vector", lambda E, sub=sub, dst=dst: E.tensor_scalar(
                    dst, sub, stat["h2s"], None, ALU.is_ge, ALU.bypass),
                    reads=[sres], sreads=[ST_R["h2s"]], writes=wres)
                kb.I("vector", lambda E, sub=sub, dst=dst: E.scalar_tensor_tensor(
                    dst, sub, stat["nh2s"], dst, ALU.is_le, ALU.subtract),
                    reads=[sres, T2A_R[i]], sreads=[ST_R["nh2s"]], writes=wres)

            def p6_tile(t):
                sl = t % 2
                par = t % 2
                if t >= 2:
                    kb.dma("sync", dsems[f"xs{sl}"],
                           lambda E, t=t, sl=sl: E.dma_start(
                               out=XS[sl][:], in_=x2s_d[128 * t:128 * (t + 1), :]),
                           reads=list(X2S_R), writes=[XS_R[sl]])
                sc_b, sc_r = (scr, SCR_R) if par == 0 else (scr2, SCR2_R)
                rk, nk = ln_stats(acc_tile(t)[:], ACC_R[t], par, trash=(sc_b, sc_r))
                ln_apply(acc_tile(t)[:], ACC_R[t], sc_b[:], sc_r, rk, nk)
                kb.I("vector", lambda E, sl=sl, sc_b=sc_b: E.tensor_tensor(
                    XS[sl][:], XS[sl][:], sc_b[:], ALU.add),
                    reads=[XS_R[sl], sc_r], writes=[XS_R[sl]])
                kb.dma("sync", dsems["store"],
                       lambda E, t=t, sl=sl: E.dma_start(
                           out=out_d[128 * t:128 * (t + 1), :], in_=XS[sl][:]),
                       reads=[XS_R[sl]], writes=[])

            t2_quant(0, 0)
            t2_quant(0, 1)
            for dg in range(4):
                for tp in range(2):
                    for i in range(NHC):
                        if dg == 0 and tp == 0 and i + 2 < NHC:
                            t2_quant(0, i + 2)
                        for tt in range(4):
                            t_glob = 4 * tp + tt
                            kb.I("tensor", lambda E, i=i, tt=tt, tg=t_glob: E.matmul(
                                pf[tt][:, 0:512], G[:, 1024 * i + 128 * tg:1024 * i + 128 * (tg + 1)],
                                XNT[:, 512 * i:512 * (i + 1)], start=(i == 0), stop=(i == NHC - 1)),
                                reads=[G_R[i], T2A_R[i]], writes=[PF_R[tt]])
                        if tp == 1 and dg < 3:
                            t2_quant(dg + 1, i)
                    for tt in range(4):
                        t_glob = 4 * tp + tt
                        kb.I("scalar", lambda E, tt=tt, tg=t_glob, dg=dg: E.activation(
                            acc_tile(tg)[:, 512 * dg:512 * (dg + 1)], pf[tt][:, 0:512], AF.Identity,
                            bias=0.0, scale=stat["ns2"]),
                            reads=[PF_R[tt]], sreads=[ST_R["ns2"]], writes=[ACC_R[t_glob]])
                    if dg == 3 and tp == 0:
                        for sl in range(2):
                            kb.dma("sync", dsems[f"xs{sl}"],
                                   lambda E, sl=sl: E.dma_start(
                                       out=XS[sl][:], in_=x2s_d[128 * sl:128 * (sl + 1), :]),
                                   reads=list(X2S_R), writes=[XS_R[sl]])
                        for t6 in range(4):
                            p6_tile(t6)
                    if dg == 3 and tp == 1:
                        for t6 in range(4, 8):
                            p6_tile(t6)


        kb.t = Tracker("scan")
        walk()
        marks = kb.t.marks
        scan_idx = dict(kb.t.idx)
        kb.t = Tracker("emit", marks)
        walk()
        assert kb.t.idx == scan_idx, "walk passes diverged"
        t = kb.t

        n_store = t.dma_n[dsems["store"]] * 16
        t.ops["sync"].append(lambda E, h=dsems["store"].h, c=n_store: E.wait_ge(h, c))
        t.abs_ops["sync"].append(("wait", "d_store", n_store))
        deadlock_check(t.abs_ops)

        with nc.Block() as block:
            for eng in ENGINES:
                getattr(block, eng)(lambda E, ops=t.ops[eng]: [f(E) for f in ops])

        stats = {e: t.idx[e] for e in ENGINES}
        stats["marks"] = len(marks)
        stats["waits"] = t.n_waits
        build_nc.stats = stats
    return nc


def kernel(L0_attn_x, L0_attn_Wq, L0_attn_bq, L0_attn_Wk, L0_attn_bk,
           L0_attn_Wv, L0_attn_bv, L0_ffn_w1, L0_ffn_w3, L0_ffn_w2):
    x = np.ascontiguousarray(np.asarray(L0_attn_x, dtype=np.float32))
    WqT = np.ascontiguousarray(np.asarray(L0_attn_Wq, dtype=np.float32).T)
    WkT = np.ascontiguousarray(np.asarray(L0_attn_Wk, dtype=np.float32).T)
    WvT = np.ascontiguousarray(np.asarray(L0_attn_Wv, dtype=np.float32).T)
    bq = np.asarray(L0_attn_bq, dtype=np.float32)
    bk = np.asarray(L0_attn_bk, dtype=np.float32)
    bv = np.asarray(L0_attn_bv, dtype=np.float32)
    w1 = np.asarray(L0_ffn_w1, dtype=np.float32)
    w3 = np.asarray(L0_ffn_w3, dtype=np.float32)
    w2 = np.asarray(L0_ffn_w2, dtype=np.float32)

    w1tp = np.zeros((D, HIDP), np.float32); w1tp[:, :HID] = w1.T
    w3tp = np.zeros((D, HIDP), np.float32); w3tp[:, :HID] = w3.T
    w2tp = np.zeros((HIDP, D), np.float32); w2tp[:HID] = w2.T
    bqs = np.ascontiguousarray(np.concatenate(
        [bq.reshape(16, 128).T, bk.reshape(16, 128).T, bv.reshape(16, 128).T], axis=1))

    nc = build_nc()
    in_maps = []
    B, N = x.shape[0], x.shape[1]
    for core in range(8):
        b, hh = core // 2, core % 2
        s = hh * TPC
        left = x[b, (s - 128) % N:(s - 128) % N + 128]
        right = x[b, (s + TPC) % N:(s + TPC) % N + 128]
        x_ext = np.ascontiguousarray(np.concatenate([left, x[b, s:s + TPC], right], axis=0))
        in_maps.append({"x": x_ext, "wqt": WqT, "wkt": WkT, "wvt": WvT, "bqs": bqs,
                        "w1t": w1tp, "w3t": w3tp, "w2t": w2tp})
    res = run_bass_kernel_spmd(nc, in_maps, core_ids=list(range(8)))
    out = np.empty((B, N, D), np.float32)
    for core in range(8):
        b, hh = core // 2, core % 2
        out[b, hh * TPC:(hh + 1) * TPC] = res.results[core]["out"]
    return out


if __name__ == "__main__":
    nc = build_nc()
    print("instructions:", build_nc.stats)


# version 25 (final)
# speedup vs baseline: 1.0228x; 1.0228x over previous
import math
from contextlib import ExitStack

import numpy as np

import concourse.bass as bass
import concourse.mybir as mybir
from concourse.bass_utils import run_bass_kernel_spmd

F32 = mybir.dt.float32
BF16 = mybir.dt.bfloat16
AF = mybir.ActivationFunctionType
ALU = mybir.AluOpType
AX = mybir.AxisListType

ENGINES = ("sync", "gpsimd", "tensor", "vector", "scalar")

D = 2048
ND = 16
TPC = 1024
NHEAD = 16
HID = 5461
HIDP = 5504
NHC = 43
NEL = float(HID * D)
EPS = 1e-5
SQRT_HD = float(np.sqrt(128).astype(np.float32))
NEG = -1000000000.0
XNT_STRIDE = 1376


class Res:
    __slots__ = ("name",)
    _all = []

    def __init__(self, name):
        self.name = name
        Res._all.append(self)


class DSem:
    __slots__ = ("name", "h")
    _all = []

    def __init__(self, name):
        self.name = name
        self.h = None
        DSem._all.append(self)


class Tracker:
    def __init__(self, mode, marks=None):
        self.mode = mode
        self.marks = marks if marks is not None else set()
        self.idx = {e: 0 for e in ENGINES}
        self.mcount = {e: 0 for e in ENGINES}
        self.depcnt = {}
        self.state = {}
        self.dma_n = {}
        self.water = {e: {} for e in ENGINES}
        self.ops = {e: [] for e in ENGINES}
        self.abs_ops = {e: [] for e in ENGINES}
        self.n_waits = 0

    def st(self, res):
        s = self.state.get(res)
        if s is None:
            s = {"w": None, "r": {}}
            self.state[res] = s
        return s


class KB:

    def __init__(self, esems):
        self.esems = esems
        self.t = None

    def _wait_dep(self, eng, dep):
        t = self.t
        if dep is None:
            return
        kind = dep[0]
        if kind == "e":
            peng, pidx = dep[1], dep[2]
            if peng == eng and False:
                return
            if t.mode == "scan":
                t.marks.add((peng, pidx))
                return
            cnt = t.depcnt.get((peng, pidx))
            assert cnt is not None, f"dep ({peng},{pidx}) not resolved"
            wm = t.water[eng]
            key = ("e", peng)
            if wm.get(key, -1) >= cnt:
                return
            wm[key] = cnt
            h = self.esems[peng]
            t.ops[eng].append(lambda E, h=h, c=cnt: E.wait_ge(h, c))
            t.abs_ops[eng].append(("wait", "e_" + peng, cnt))
            t.n_waits += 1
        else:
            dsem, tgt = dep[1], dep[2]
            if t.mode == "scan":
                assert tgt <= t.dma_n[dsem] * 16, f"partial-cumulative DMA wait on {dsem.name}"
                return
            wm = t.water[eng]
            key = ("d", dsem)
            if wm.get(key, -1) >= tgt:
                return
            wm[key] = tgt
            t.ops[eng].append(lambda E, h=dsem.h, c=tgt: E.wait_ge(h, c))
            t.abs_ops[eng].append(("wait", "d_" + dsem.name, tgt))
            t.n_waits += 1

    def I(self, eng, f, reads=(), writes=(), sreads=()):
        t = self.t
        idx = t.idx[eng]
        for r in reads:
            s = t.st(r)
            w = s["w"]
            if w is not None:
                self._wait_dep(eng, w)
        for r in sreads:
            s = t.st(r)
            w = s["w"]
            if w is not None:
                self._wait_dep(eng, w)
        for r in writes:
            s = t.st(r)
            w = s["w"]
            if w is not None and not (w[0] == "e" and w[1] == eng):
                self._wait_dep(eng, w)
            for rd in s["r"].values():
                if not (rd[0] == "e" and rd[1] == eng):
                    self._wait_dep(eng, rd)
        dep = ("e", eng, idx)
        if t.mode == "emit":
            marked = (eng, idx) in t.marks
            if marked:
                t.mcount[eng] += 1
                h = self.esems[eng]
                c = t.mcount[eng]
                t.ops[eng].append(lambda E, f=f, h=h, c=c: f(E).then_inc(h, 1))
                t.abs_ops[eng].append(("inc", "e_" + eng, 1))
            else:
                t.ops[eng].append(lambda E, f=f: f(E))
                t.abs_ops[eng].append(("instr",))
            t.depcnt[(eng, idx)] = t.mcount[eng]
        t.idx[eng] = idx + 1
        for r in reads:
            t.st(r)["r"][("e", eng)] = dep
        for r in sreads:
            t.st(r)["r"][("e", eng)] = dep
        for r in writes:
            s = t.st(r)
            s["w"] = dep
            s["r"] = {}
        return dep

    def dma(self, eng, dsem, f, reads=(), writes=()):
        t = self.t
        for r in reads:
            w = t.st(r)["w"]
            self._wait_dep(eng, w)
        for r in writes:
            s = t.st(r)
            self._wait_dep(eng, s["w"])
            for rd in s["r"].values():
                self._wait_dep(eng, rd)
        t.dma_n[dsem] = t.dma_n.get(dsem, 0) + 1
        tgt = t.dma_n[dsem] * 16
        dep = ("d", dsem, tgt)
        if t.mode == "emit":
            t.ops[eng].append(lambda E, f=f, h=dsem.h: f(E).then_inc(h, 16))
            t.abs_ops[eng].append(("inc", "d_" + dsem.name, 16))
        t.idx[eng] += 1
        for r in reads:
            t.st(r)["r"][("d", dsem)] = dep
        for r in writes:
            s = t.st(r)
            s["w"] = dep
            s["r"] = {}
        return dep


def deadlock_check(abs_ops):
    pos = {e: 0 for e in ENGINES}
    sems = {}
    progress = True
    while progress:
        progress = False
        for e in ENGINES:
            while pos[e] < len(abs_ops[e]):
                op = abs_ops[e][pos[e]]
                if op[0] == "wait" and sems.get(op[1], 0) < op[2]:
                    break
                if op[0] == "inc":
                    sems[op[1]] = sems.get(op[1], 0) + op[2]
                pos[e] += 1
                progress = True
    stuck = {e: (pos[e], len(abs_ops[e]), abs_ops[e][pos[e]] if pos[e] < len(abs_ops[e]) else None)
             for e in ENGINES if pos[e] < len(abs_ops[e])}
    if stuck:
        raise RuntimeError(f"DEADLOCK: {stuck}")


def q_chunk_table(h):
    out = []
    if h < 8:
        for a in range(4):
            kb = 128 + 256 * a
            out.append((kb, kb, 0))
            out.append((kb + 128, kb, 1))
    else:
        out.append((128, 0, 1))
        for a in range(1, 4):
            out.append((256 * a, 256 * a, 0))
            out.append((256 * a + 128, 256 * a, 1))
        out.append((1024, 1024, 0))
    return out


def build_nc():
    nc = bass.Bass("TRN2", target_bir_lowering=False)

    x_in = nc.dram_tensor("x", [1280, D], F32, kind="ExternalInput").ap()
    wq_d = nc.dram_tensor("wqt", [D, D], F32, kind="ExternalInput").ap()
    wk_d = nc.dram_tensor("wkt", [D, D], F32, kind="ExternalInput").ap()
    wv_d = nc.dram_tensor("wvt", [D, D], F32, kind="ExternalInput").ap()
    bqs_d = nc.dram_tensor("bqs", [128, 48], F32, kind="ExternalInput").ap()
    w1t_d = nc.dram_tensor("w1t", [D, HIDP], F32, kind="ExternalInput").ap()
    w3t_d = nc.dram_tensor("w3t", [D, HIDP], F32, kind="ExternalInput").ap()
    w2t_d = nc.dram_tensor("w2t", [HIDP, D], F32, kind="ExternalInput").ap()
    out_d = nc.dram_tensor("out", [TPC, D], F32, kind="ExternalOutput").ap()
    x2s_d = nc.dram_tensor("x2s", [TPC, D], F32, kind="Internal").ap()

    with ExitStack() as ctx:
        E = ctx.enter_context

        ident = E(nc.sbuf_tensor("ident", [128, 128], BF16))
        mask_lo = E(nc.sbuf_tensor("mask_lo", [128, 256], BF16))
        mask_hi = E(nc.sbuf_tensor("mask_hi", [128, 256], BF16))
        ones_c = E(nc.sbuf_tensor("ones_c", [128, 1], F32))
        ones_r = E(nc.sbuf_tensor("ones_r", [1, 128], F32))
        bqs = E(nc.sbuf_tensor("bqs_sb", [128, 48], F32))
        c128 = E(nc.sbuf_tensor("c128", [128, 1], F32))
        rc128 = E(nc.sbuf_tensor("rc128", [128, 1], F32))
        cnel = E(nc.sbuf_tensor("cnel", [128, 1], F32))
        rnel = E(nc.sbuf_tensor("rnel", [128, 1], F32))
        XNT = E(nc.sbuf_tensor("XNT", [128, ND * XNT_STRIDE], BF16))
        ACC = E(nc.sbuf_tensor("ACC", [128, 8 * D], BF16))
        G = E(nc.sbuf_tensor("G", [128, NHC * 1024], BF16))
        XS = [E(nc.sbuf_tensor(f"XS{i}", [128, D], F32)) for i in range(2)]
        WB = [E(nc.sbuf_tensor(f"WB{i}", [128, 1024], F32)) for i in range(4)]
        S_sb = [E(nc.sbuf_tensor(f"S{i}", [128, 256], F32)) for i in range(4)]
        scr = E(nc.sbuf_tensor("scr", [128, D], BF16))
        scr2 = E(nc.sbuf_tensor("scr2", [128, D], BF16))
        _stat_keys = ["sum0", "sum1", "ss0", "ss1", "m0", "m1", "nm20", "nm21", "ve0", "ve1",
                      "r10", "r11", "rstd0", "rstd1", "nmr0", "nmr1",
                      "smax0", "smax1", "smax2", "smax3",
                      "rsum0", "rsum1", "rsum2", "rsum3", "rsum4", "rsum5", "rsum6", "rsum7",
                      "rrec0", "rrec1", "rrec2", "rrec3", "rrec4", "rrec5", "rrec6", "rrec7",
                      "acc1v", "acc3v", "acc2v", "acc1s", "acc3s", "acc2s",
                      "acc1", "acc3", "acc2", "tmpr", "tmps",
                      "s1b", "s3b", "s2b", "h1s", "h3s", "h2s", "nh1s", "nh3s", "nh2s",
                      "ns1", "ns3", "ns2", "nshift"]
        STATS = E(nc.sbuf_tensor("STATS", [128, len(_stat_keys)], F32))
        stat = {k: STATS[:, i:i + 1] for i, k in enumerate(_stat_keys)}
        tot = E(nc.sbuf_tensor("tot", [1, 3], F32))

        pf = [E(nc.psum_tensor(f"pf{i}", [128, 512], F32)) for i in range(6)]
        pb = [E(nc.psum_tensor(f"pbk{i}", [128, 1024], BF16)) for i in range(2)]

        esems = {}
        for e in ENGINES:
            esems[e] = E(nc.semaphore(f"ev_{e}"))
        dsems = {}
        for name in ["xs0", "xs1", "ws0", "ws1", "w30", "w31", "wn00", "wn01", "wn02",
                     "wn10", "wn11", "wn12", "misc", "spill", "store",
                     "xt", "vt0", "vt1", "vt2",
                     "pw0", "pw1", "pw2", "pw3", "pw4", "pw5", "pw6", "pw7"]:
            ds = DSem(name)
            ds.h = E(nc.semaphore(f"dma_{name}"))
            dsems[name] = ds

        R = {}

        def res(name):
            if name not in R:
                R[name] = Res(name)
            return R[name]

        def g_wt(par, mat):
            return G[:, 6144 * par + 2048 * mat: 6144 * par + 2048 * (mat + 1)]

        def g_qt(r):
            return G[:, 24576 + 3584 * r: 24576 + 3584 * r + 1024]

        def g_kt(r):
            return G[:, 24576 + 3584 * r + 1024: 24576 + 3584 * r + 2304]

        def g_vtok(r):
            return G[:, 24576 + 3584 * r + 2304: 24576 + 3584 * r + 3584]

        def g_vt(par):
            return G[:, 35328 + 1280 * par: 35328 + 1280 * (par + 1)]

        def g_p(par, k):
            return G[:, 37888 + 2048 * par + 256 * k: 37888 + 2048 * par + 256 * (k + 1)]

        def g_pt(par, k):
            return G[:, 41984 + 1024 * par + 256 * k: 41984 + 1024 * par + 256 * (k + 1)]

        def acc_tile(t):
            return ACC[:, 2048 * t: 2048 * (t + 1)]

        q_t1t = [ACC[:, 2048 * s:2048 * (s + 1)] for s in range(2)]
        q_t3t = [ACC[:, 4096 + 2048 * s:4096 + 2048 * (s + 1)] for s in range(2)]
        q_g1 = scr[:, 0:1024]

        def xnt_chunk(d):
            return XNT[:, XNT_STRIDE * d: XNT_STRIDE * d + 1280]

        def t2nat(j):
            return XNT[:, 5504 * j: 5504 * (j + 1)]

        kb = KB(esems)

        ID_R = res("ident"); MLO = res("mlo"); MHI = res("mhi")
        ONESC = res("onesc"); ONESR = res("onesr"); BQS = res("bqs"); BQSC = res("bq_s")
        RC = res("rc128"); C128R = res("c128raw"); RNEL = res("rnel"); CNEL = res("cnel")
        XNT_R = [res(f"xnt{d}") for d in range(ND)]
        ACC_R = [res(f"acc{t}") for t in range(8)]
        QT1T_R = [res(f"qt1t{s}") for s in range(2)]
        QT3T_R = [res(f"qt3t{s}") for s in range(2)]
        QG1_R = res("qg1")
        WBR = [res(f"wb{i}") for i in range(8)]
        G_R = [res(f"g{i}") for i in range(NHC)]
        GWT_R = [[res(f"gwt{p}_{m}") for m in range(3)] for p in range(2)]
        GQT_R = [res(f"gqt{r}") for r in range(3)]
        GKT_R = [res(f"gkt{r}") for r in range(3)]
        GVK_R = [res(f"gvk{r}") for r in range(3)]
        GVT_R = [res(f"gvt{p}") for p in range(2)]
        GP_R = [[res(f"gp{p}_{k}") for k in range(8)] for p in range(2)]
        GPT_R = [[res(f"gpt{p}_{k}") for k in range(4)] for p in range(2)]
        XS_R = [res(f"xs{i}") for i in range(2)]
        SSB_R = [res(f"ssb{i}") for i in range(4)]
        SCR_R = res("scr")
        SCR2_R = res("scr2")
        PF_R = [res(f"pf{i}") for i in range(6)]
        PB_R = [res(f"pbk{i}") for i in range(2)]
        ST_R = {k: res(f"st_{k}") for k in stat}
        TOT_R = res("tot")
        X2S_R = [res(f"x2s{t}") for t in range(8)]

        def walk():
            kb.I("gpsimd", lambda E: E.memset(ident[:], 0.0), writes=[ID_R])
            kb.I("gpsimd", lambda E: E.affine_select(
                out=ident[:], in_=ident[:], compare_op=ALU.not_equal, fill=1.0,
                base=0, pattern=[[-1, 128]], channel_multiplier=1),
                reads=[ID_R], writes=[ID_R])
            kb.I("gpsimd", lambda E: E.memset(mask_lo[:], 0.0), writes=[MLO])
            kb.I("gpsimd", lambda E: E.affine_select(
                out=mask_lo[:], in_=mask_lo[:], compare_op=ALU.is_ge, fill=NEG,
                base=0, pattern=[[-1, 256]], channel_multiplier=1),
                reads=[MLO], writes=[MLO])
            kb.I("gpsimd", lambda E: E.memset(mask_hi[:], 0.0), writes=[MHI])
            kb.I("gpsimd", lambda E: E.affine_select(
                out=mask_hi[:], in_=mask_hi[:], compare_op=ALU.is_ge, fill=NEG,
                base=128, pattern=[[-1, 256]], channel_multiplier=1),
                reads=[MHI], writes=[MHI])
            kb.I("gpsimd", lambda E: E.memset(ones_c[:], 1.0), writes=[ONESC])
            kb.I("gpsimd", lambda E: E.memset(ones_r[:], 1.0), writes=[ONESR])
            for ak in ("acc1v", "acc3v", "acc2v", "acc1s", "acc3s", "acc2s"):
                kb.I("gpsimd", lambda E, ak=ak: E.memset(stat[ak], 0.0), writes=[ST_R[ak]])
            kb.I("vector", lambda E: E.memset(stat["nshift"], -20.0), writes=[ST_R["nshift"]])
            kb.I("vector", lambda E: E.memset(c128[:], SQRT_HD), writes=[C128R])
            kb.I("vector", lambda E: E.reciprocal(rc128[:], c128[:]), reads=[C128R], writes=[RC])
            kb.I("vector", lambda E: E.memset(cnel[:], NEL), writes=[CNEL])
            kb.I("vector", lambda E: E.reciprocal(rnel[:], cnel[:]), reads=[CNEL], writes=[RNEL])
            kb.dma("sync", dsems["misc"], lambda E: E.dma_start(out=bqs[:], in_=bqs_d[:]),
                   writes=[BQS])

            def ln_stats(src_ap, src_res, par, extra_reads=(), trash=None):
                tb, tr = trash if trash is not None else (scr, SCR_R)
                p = str(par)
                kb.I("vector", lambda E, a=src_ap: E.reduce_sum(stat["sum" + p], a, axis=AX.X),
                     reads=[src_res, *extra_reads], writes=[ST_R["sum" + p]])
                kb.I("scalar", lambda E, a=src_ap, tb=tb: E.activation(
                    tb[:], a, AF.Square, accum_out=stat["ss" + p]),
                    reads=[src_res, *extra_reads], writes=[tr, ST_R["ss" + p]])
                kb.I("vector", lambda E: E.tensor_scalar(
                    stat["m" + p], stat["sum" + p], 1.0 / D, None, ALU.mult, ALU.bypass),
                    reads=[ST_R["sum" + p]], writes=[ST_R["m" + p]])
                kb.I("vector", lambda E: E.tensor_scalar(
                    stat["nm2" + p], stat["m" + p], stat["m" + p], -1.0, ALU.mult, ALU.mult),
                    reads=[ST_R["m" + p]], sreads=[ST_R["m" + p]], writes=[ST_R["nm2" + p]])
                kb.I("vector", lambda E: E.scalar_tensor_tensor(
                    stat["ve" + p], stat["ss" + p], 1.0 / D, stat["nm2" + p], ALU.mult, ALU.add),
                    reads=[ST_R["ss" + p], ST_R["nm2" + p]], writes=[ST_R["ve" + p]])
                kb.I("vector", lambda E: E.tensor_scalar(
                    stat["ve" + p], stat["ve" + p], EPS, None, ALU.add, ALU.bypass),
                    reads=[ST_R["ve" + p]], writes=[ST_R["ve" + p]])
                kb.I("vector", lambda E: E.reciprocal(stat["r1" + p], stat["ve" + p]),
                     reads=[ST_R["ve" + p]], writes=[ST_R["r1" + p]])
                kb.I("scalar", lambda E: E.activation(stat["rstd" + p], stat["r1" + p], AF.Sqrt),
                     reads=[ST_R["r1" + p]], writes=[ST_R["rstd" + p]])
                kb.I("vector", lambda E: E.tensor_scalar(
                    stat["nmr" + p], stat["m" + p], stat["rstd" + p], -1.0, ALU.mult, ALU.mult),
                    reads=[ST_R["m" + p]], sreads=[ST_R["rstd" + p]], writes=[ST_R["nmr" + p]])
                return "rstd" + p, "nmr" + p

            def ln_apply(src_ap, src_res, dst_ap, dst_res, rk, nk, dtypeout_bf16=True):
                kb.I("scalar", lambda E, a=src_ap, o=dst_ap: E.activation(
                    o, a, AF.Identity, bias=stat[nk], scale=stat[rk]),
                    reads=[src_res], sreads=[ST_R[rk], ST_R[nk]], writes=[dst_res])

            xr3 = XNT.ap().rearrange("p (d c) -> p d c", c=XNT_STRIDE)
            gr128 = G.ap().rearrange("p (c f) -> p c f", f=128)

            def xpose_tile_to_xnt(src_buf, src_res, col0):
                kb.dma("sync", dsems["xt"],
                       lambda E, s=src_buf, c=col0: E.dma_start(
                           out=xr3[:, :, c:c + 128], in_=s[:], transpose=True),
                       reads=[src_res], writes=list(XNT_R))

            def wb_slot(k):
                k = k % 8
                return WB[k // 2][:, 512 * (k % 2):512 * (k % 2) + 512], WBR[k]

            phaseA = []
            for mat, src_d, acck in ((0, w1t_d, "acc1"), (1, w3t_d, "acc3")):
                for a in range(ND):
                    for o in range(0, HIDP, 512):
                        w = min(512, HIDP - o)
                        phaseA.append((src_d[128 * a:128 * (a + 1), o:o + w], w, acck))
            for i in range(NHC):
                for o in range(0, D, 512):
                    phaseA.append((w2t_d[128 * i:128 * (i + 1), o:o + 512], 512, "acc2"))
            _engs = ["vector"] * 40 + [("scalar", "vector")[i % 2] for i in range(len(phaseA) - 40)]
            phaseA = [(s, w, a, e) for (s, w, a), e in zip(phaseA, _engs)]
            pa_iter = iter(phaseA)
            pa_state = {"k": 0}

            def emit_phaseA(n):
                for _ in range(n):
                    item = next(pa_iter, None)
                    if item is None:
                        return
                    src, w, acck, eng = item
                    acck = acck + ("v" if eng == "vector" else "s")
                    k = pa_state["k"]
                    pa_state["k"] += 1
                    sub, sres = wb_slot(k)
                    tk = "tmpr" if eng == "vector" else "tmps"
                    kb.dma("gpsimd" if k % 2 == 0 else "sync", dsems[f"pw{k % 8}"],
                           lambda E, s=src, w=w, sub=sub: E.dma_start(out=sub[:, 0:w], in_=s),
                           writes=[sres])
                    if eng == "vector":
                        kb.I("vector", lambda E, w=w, sub=sub, tk=tk: E.tensor_reduce(
                            stat[tk], sub[:, 0:w], AX.X, ALU.add, apply_absolute_value=True),
                            reads=[sres], writes=[ST_R[tk]])
                        kb.I("vector", lambda E, a=acck, tk=tk: E.tensor_tensor(
                            stat[a], stat[a], stat[tk], ALU.add),
                            reads=[ST_R[acck], ST_R[tk]], writes=[ST_R[acck]])
                    else:
                        kb.I("scalar", lambda E, w=w, sub=sub, tk=tk: E.activation(
                            scr[:, 0:w], sub[:, 0:w], AF.Abs, accum_out=stat[tk]),
                            reads=[sres], writes=[SCR_R, ST_R[tk]])
                        kb.I("scalar", lambda E, a=acck, tk=tk: E.activation(
                            stat[a], stat[a], AF.Identity, bias=stat[tk], scale=1.0),
                            reads=[ST_R[acck]], sreads=[ST_R[tk]], writes=[ST_R[acck]])

            def wload(h):
                par = h % 2
                for mi, wd in enumerate((wq_d, wk_d, wv_d)):
                    b0 = (6144 * par + 2048 * mi) // 128
                    kb.dma("gpsimd", dsems[f"wn{par}{mi}"],
                           lambda E, wd=wd, h=h, b0=b0: E.dma_start(
                               out=gr128[:, b0:b0 + 16, :],
                               in_=wd[:, 128 * h:128 * (h + 1)]
                               .rearrange("(a p) c -> p a c", p=128)),
                           writes=[GWT_R[par][mi]])

            mblocks = [(0, 128, 512, "q", 0), (0, 640, 512, "q", 512),
                       (1, 0, 512, "k", 0), (1, 512, 512, "k", 512), (1, 1024, 256, "k", 1024),
                       (2, 0, 512, "v", 0), (2, 512, 512, "v", 512), (2, 1024, 256, "v", 1024)]

            def m_block(h, bi):
                par = h % 2
                r = h % 3
                mat, c0, w, kind, d0 = mblocks[bi]
                bank = bi % 2
                for d in range(ND):
                    kb.I("tensor", lambda E, par=par, mat=mat, d=d, c0=c0, w=w, b=bank: E.matmul(
                        pf[b][:, 0:w], g_wt(par, mat)[:, 128 * d:128 * (d + 1)],
                        xnt_chunk(d)[:, c0:c0 + w], start=(d == 0), stop=(d == ND - 1)),
                        reads=[GWT_R[par][mat], XNT_R[d]], writes=[PF_R[bank]])
                if kind == "q":
                    dst, dres = g_qt(r), GQT_R[r]
                elif kind == "k":
                    dst, dres = g_kt(r), GKT_R[r]
                else:
                    dst, dres = g_vt(par), GVT_R[par]
                if bi % 2 == 0:
                    kb.I("vector", lambda E, dst=dst, d0=d0, w=w, b=bank, mat=mat, h=h: E.tensor_scalar(
                        dst[:, d0:d0 + w], pf[b][:, 0:w],
                        bqs[:, 16 * mat + h:16 * mat + h + 1], None, ALU.add, ALU.bypass),
                        reads=[PF_R[bank]], sreads=[BQS], writes=[dres])
                else:
                    kb.I("scalar", lambda E, dst=dst, d0=d0, w=w, b=bank, mat=mat, h=h: E.activation(
                        dst[:, d0:d0 + w], pf[b][:, 0:w], AF.Identity,
                        bias=bqs[:, 16 * mat + h:16 * mat + h + 1], scale=1.0),
                        reads=[PF_R[bank], BQS], writes=[dres])

            def v_tpose(h):
                par = h % 2
                r = h % 3
                b0 = (24576 + 3584 * r + 2304) // 128
                kb.dma("sync", dsems[f"vt{r}"],
                       lambda E, par=par, b0=b0: E.dma_start(
                           out=gr128[:, b0:b0 + 10, :], in_=g_vt(par)[:], transpose=True),
                       reads=[GVT_R[par]], writes=[GVK_R[r]])

            def s_half(h, qi):
                par = h % 2
                r = h % 3
                qc0, kb0, msel = q_chunk_table(h)[qi]
                sp = qi % 4
                sbank = 2 + (qi % 2)
                mres = MLO if msel == 0 else MHI
                mask = mask_lo if msel == 0 else mask_hi
                kb.I("tensor", lambda E, r=r, qc0=qc0, kb0=kb0, sb=sbank: E.matmul(
                    pf[sb][:, 0:256], g_qt(r)[:, qc0 - 128:qc0], g_kt(r)[:, kb0:kb0 + 256],
                    start=True, stop=True),
                    reads=[GQT_R[r], GKT_R[r]], writes=[PF_R[sbank]])
                kb.I("vector", lambda E, sp=sp, sb=sbank, m=mask: E.scalar_tensor_tensor(
                    S_sb[sp][:], pf[sb][:, 0:256], rc128[:, 0:1], m[:], ALU.mult, ALU.add),
                    reads=[PF_R[sbank], mres], sreads=[RC], writes=[SSB_R[sp]])
                kb.I("scalar", lambda E, sp=sp, qi=qi, par=par: E.activation(
                    g_p(par, qi)[:], S_sb[sp][:], AF.Exp, bias=stat["nshift"],
                    accum_out=stat[f"rsum{qi}"]),
                    reads=[SSB_R[sp]], sreads=[ST_R["nshift"]],
                    writes=[GP_R[par][qi], ST_R[f"rsum{qi}"]])
                kb.I("vector", lambda E, qi=qi: E.reciprocal(
                    stat[f"rrec{qi}"], stat[f"rsum{qi}"]),
                    reads=[ST_R[f"rsum{qi}"]], writes=[ST_R[f"rrec{qi}"]])

            def ptt(h, qi):
                par = h % 2
                sp = qi % 4
                pbank = qi % 2
                for kc in range(2):
                    kb.I("tensor", lambda E, par=par, qi=qi, kc=kc, b=pbank: E.transpose(
                        pb[b][:, 512 + 128 * kc:512 + 128 * (kc + 1)],
                        g_p(par, qi)[:, 128 * kc:128 * (kc + 1)], ident[:]),
                        reads=[GP_R[par][qi], ID_R], writes=[PB_R[pbank]])
                kb.I("vector", lambda E, par=par, sp=sp, b=pbank: E.tensor_copy(
                    g_pt(par, sp)[:], pb[b][:, 512:768]),
                    reads=[PB_R[pbank]], writes=[GPT_R[par][sp]])

            def o_mm(h, qi):
                par = h % 2
                r = h % 3
                qc0, kb0, msel = q_chunk_table(h)[qi]
                sp = qi % 4
                obank = 4 + (qi % 2)
                for kc in range(2):
                    kb.I("tensor", lambda E, par=par, r=r, sp=sp, kc=kc, ob=obank, kb0=kb0: E.matmul(
                        pf[ob][:, 0:128], g_pt(par, sp)[:, 128 * kc:128 * (kc + 1)],
                        g_vtok(r)[:, kb0 + 128 * kc:kb0 + 128 * (kc + 1)],
                        start=(kc == 0), stop=(kc == 1)),
                        reads=[GPT_R[par][sp], GVK_R[r]], writes=[PF_R[obank]])
                tt = qc0 // 128 - 1
                kb.I("vector", lambda E, qi=qi, ob=obank, tt=tt, h=h: E.tensor_scalar(
                    acc_tile(tt)[:, 128 * h:128 * (h + 1)], pf[ob][:, 0:128],
                    stat[f"rrec{qi}"], None, ALU.mult, ALU.bypass),
                    reads=[PF_R[obank]], sreads=[ST_R[f"rrec{qi}"]], writes=[ACC_R[tt]])

            wload(0)
            wload(1)
            for k in range(10):
                sl = k % 2
                kb.dma("gpsimd", dsems[f"xs{sl}"],
                       lambda E, k=k, sl=sl: E.dma_start(out=XS[sl][:], in_=x_in[128 * k:128 * (k + 1), :]),
                       writes=[XS_R[sl]])
                par = k % 2
                sc_b, sc_r = (scr, SCR_R) if par == 0 else (scr2, SCR2_R)
                rk, nk = ln_stats(XS[sl][:], XS_R[sl], par, trash=(sc_b, sc_r))
                ln_apply(XS[sl][:], XS_R[sl], sc_b[:], sc_r, rk, nk)
                xpose_tile_to_xnt(sc_b, sc_r, 128 * k)
                emit_phaseA(4)

            for h in range(NHEAD):
                if h >= 2:
                    o_mm(h - 2, 7)
                for bi in range(8):
                    m_block(h, bi)
                    if h >= 1:
                        if bi >= 2:
                            ptt(h - 1, bi - 2)
                        s_half(h - 1, bi)
                        if bi >= 3:
                            o_mm(h - 1, bi - 3)
                    emit_phaseA(4)
                v_tpose(h)
                if h + 2 < NHEAD:
                    wload(h + 2)
                if h >= 1:
                    ptt(h - 1, 6)
                    o_mm(h - 1, 5)
                    ptt(h - 1, 7)
                    o_mm(h - 1, 6)
                emit_phaseA(1)
            o_mm(NHEAD - 2, 7)
            for qi in range(8):
                s_half(NHEAD - 1, qi)
                emit_phaseA(1)
                if qi >= 2:
                    ptt(NHEAD - 1, qi - 2)
                if qi >= 3:
                    o_mm(NHEAD - 1, qi - 3)
            ptt(NHEAD - 1, 6)
            o_mm(NHEAD - 1, 5)
            ptt(NHEAD - 1, 7)
            o_mm(NHEAD - 1, 6)
            o_mm(NHEAD - 1, 7)

            emit_phaseA(10000)

            for a in ("acc1", "acc3", "acc2"):
                kb.I("vector", lambda E, a=a: E.tensor_tensor(
                    stat[a], stat[a + "v"], stat[a + "s"], ALU.add),
                    reads=[ST_R[a + "v"], ST_R[a + "s"]], writes=[ST_R[a]])

            for ci, (acck, sk, hk, nhk, nsk) in enumerate((
                    ("acc1", "s1b", "h1s", "nh1s", "ns1"),
                    ("acc3", "s3b", "h3s", "nh3s", "ns3"),
                    ("acc2", "s2b", "h2s", "nh2s", "ns2"))):
                kb.I("tensor", lambda E, a=acck, ci=ci: E.matmul(
                    pf[0][0:1, ci:ci + 1], stat[a], ones_c[:], start=True, stop=True),
                    reads=[ST_R[acck], ONESC], writes=[PF_R[0]])
            kb.I("vector", lambda E: E.tensor_copy(tot[:, 0:3], pf[0][0:1, 0:3]),
                 reads=[PF_R[0]], writes=[TOT_R])
            for ci, (acck, sk, hk, nhk, nsk) in enumerate((
                    ("acc1", "s1b", "h1s", "nh1s", "ns1"),
                    ("acc3", "s3b", "h3s", "nh3s", "ns3"),
                    ("acc2", "s2b", "h2s", "nh2s", "ns2"))):
                kb.I("tensor", lambda E, ci=ci: E.matmul(
                    pf[1][:, ci:ci + 1], ones_r[:], tot[0:1, ci:ci + 1], start=True, stop=True),
                    reads=[TOT_R, ONESR], writes=[PF_R[1]])
            for ci, (acck, sk, hk, nhk, nsk) in enumerate((
                    ("acc1", "s1b", "h1s", "nh1s", "ns1"),
                    ("acc3", "s3b", "h3s", "nh3s", "ns3"),
                    ("acc2", "s2b", "h2s", "nh2s", "ns2"))):
                kb.I("vector", lambda E, ci=ci, sk=sk: E.tensor_scalar(
                    stat[sk], pf[1][:, ci:ci + 1], rnel[:, 0:1], None, ALU.mult, ALU.bypass),
                    reads=[PF_R[1]], sreads=[RNEL], writes=[ST_R[sk]])
                kb.I("vector", lambda E, sk=sk, hk=hk: E.tensor_scalar(
                    stat[hk], stat[sk], 0.5, None, ALU.mult, ALU.bypass),
                    reads=[ST_R[sk]], writes=[ST_R[hk]])
                kb.I("vector", lambda E, sk=sk, nhk=nhk: E.tensor_scalar(
                    stat[nhk], stat[sk], -0.5, None, ALU.mult, ALU.bypass),
                    reads=[ST_R[sk]], writes=[ST_R[nhk]])
                kb.I("vector", lambda E, sk=sk, nsk=nsk: E.tensor_scalar(
                    stat[nsk], stat[sk], -1.0, None, ALU.mult, ALU.bypass),
                    reads=[ST_R[sk]], writes=[ST_R[nsk]])

            wb3 = [WB[j].ap().rearrange("p (a c) -> p a c", c=128) for j in range(4)]

            def p4_load(i):
                for h in range(2):
                    kb.dma("sync", dsems[f"ws{h}"],
                           lambda E, i=i, h=h: E.dma_start(
                               out=wb3[h][:],
                               in_=w1t_d[1024 * h:1024 * (h + 1), 128 * i:128 * (i + 1)]
                               .rearrange("(a p) c -> p a c", p=128)),
                           writes=[WBR[2 * h], WBR[2 * h + 1]])
                    kb.dma("gpsimd", dsems[f"w3{h}"],
                           lambda E, i=i, h=h: E.dma_start(
                               out=wb3[2 + h][:],
                               in_=w3t_d[1024 * h:1024 * (h + 1), 128 * i:128 * (i + 1)]
                               .rearrange("(a p) c -> p a c", p=128)),
                           writes=[WBR[4 + 2 * h], WBR[4 + 2 * h + 1]])

            def p4_quant(i):
                sl = i % 2
                for h in range(2):
                    d1 = q_t1t[sl][:, 1024 * h:1024 * (h + 1)]
                    kb.I("vector", lambda E, h=h, d1=d1: E.tensor_scalar(
                        d1, WB[h][:], stat["h1s"], None, ALU.is_ge, ALU.bypass),
                        reads=[WBR[2 * h], WBR[2 * h + 1]], sreads=[ST_R["h1s"]],
                        writes=[QT1T_R[sl], ACC_R[sl]])
                    kb.I("vector", lambda E, h=h, d1=d1: E.scalar_tensor_tensor(
                        d1, WB[h][:], stat["nh1s"], d1, ALU.is_le, ALU.subtract),
                        reads=[WBR[2 * h], WBR[2 * h + 1], QT1T_R[sl]], sreads=[ST_R["nh1s"]],
                        writes=[QT1T_R[sl], ACC_R[sl]])
                    d3 = q_t3t[sl][:, 1024 * h:1024 * (h + 1)]
                    kb.I("vector", lambda E, h=h, d3=d3: E.tensor_scalar(
                        d3, WB[2 + h][:], stat["h3s"], None, ALU.is_ge, ALU.bypass),
                        reads=[WBR[4 + 2 * h], WBR[4 + 2 * h + 1]], sreads=[ST_R["h3s"]],
                        writes=[QT3T_R[sl], ACC_R[2 + sl]])
                    kb.I("vector", lambda E, h=h, d3=d3: E.scalar_tensor_tensor(
                        d3, WB[2 + h][:], stat["nh3s"], d3, ALU.is_le, ALU.subtract),
                        reads=[WBR[4 + 2 * h], WBR[4 + 2 * h + 1], QT3T_R[sl]], sreads=[ST_R["nh3s"]],
                        writes=[QT3T_R[sl], ACC_R[2 + sl]])

            def p4_mms(i):
                sl = i % 2
                for d in range(ND):
                    for th in range(2):
                        kb.I("tensor", lambda E, sl=sl, d=d, th=th: E.matmul(
                            pf[th][:, 0:512], q_t1t[sl][:, 128 * d:128 * (d + 1)],
                            XNT[:, XNT_STRIDE * d + 512 * th: XNT_STRIDE * d + 512 * (th + 1)],
                            start=(d == 0), stop=(d == ND - 1)),
                            reads=[QT1T_R[sl], XNT_R[d]], writes=[PF_R[th]])
                for d in range(ND):
                    for th in range(2):
                        kb.I("tensor", lambda E, sl=sl, d=d, th=th: E.matmul(
                            pf[2 + th][:, 0:512], q_t3t[sl][:, 128 * d:128 * (d + 1)],
                            XNT[:, XNT_STRIDE * d + 512 * th: XNT_STRIDE * d + 512 * (th + 1)],
                            start=(d == 0), stop=(d == ND - 1)),
                            reads=[QT3T_R[sl], XNT_R[d]], writes=[PF_R[2 + th]])

            def p4_post(i):
                for th in range(2):
                    kb.I("scalar", lambda E, th=th: E.activation(
                        q_g1[:, 512 * th:512 * (th + 1)], pf[th][:, 0:512], AF.Silu,
                        bias=0.0, scale=stat["ns1"]),
                        reads=[PF_R[th]], sreads=[ST_R["ns1"]], writes=[QG1_R, SCR_R])
                for th in range(2):
                    kb.I("vector", lambda E, i=i, th=th: E.scalar_tensor_tensor(
                        G[:, 1024 * i + 512 * th:1024 * i + 512 * (th + 1)],
                        pf[2 + th][:, 0:512], stat["ns3"],
                        q_g1[:, 512 * th:512 * (th + 1)], ALU.mult, ALU.mult),
                        reads=[PF_R[2 + th], QG1_R], sreads=[ST_R["ns3"]], writes=[G_R[i]])

            for t in range(8):
                sl = t % 2
                par = t % 2
                kb.dma("sync" if t % 2 else "gpsimd", dsems[f"xs{sl}"],
                       lambda E, t=t, sl=sl: E.dma_start(
                           out=XS[sl][:], in_=x_in[128 * (t + 1):128 * (t + 2), :]),
                       writes=[XS_R[sl]])
                sc_b, sc_r = (scr, SCR_R) if par == 0 else (scr2, SCR2_R)
                rk, nk = ln_stats(acc_tile(t)[:], ACC_R[t], par, trash=(sc_b, sc_r))
                ln_apply(acc_tile(t)[:], ACC_R[t], sc_b[:], sc_r, rk, nk)
                kb.I("vector", lambda E, sl=sl, sc_b=sc_b: E.tensor_tensor(
                    XS[sl][:], XS[sl][:], sc_b[:], ALU.add),
                    reads=[XS_R[sl], sc_r], writes=[XS_R[sl]])
                rk3, nk3 = ln_stats(XS[sl][:], XS_R[sl], par, trash=(sc_b, sc_r))
                ln_apply(XS[sl][:], XS_R[sl], sc_b[:], sc_r, rk3, nk3)
                xpose_tile_to_xnt(sc_b, sc_r, 128 * t)
                kb.dma("gpsimd", dsems["spill"],
                       lambda E, t=t, sl=sl: E.dma_start(
                           out=x2s_d[128 * t:128 * (t + 1), :], in_=XS[sl][:]),
                       reads=[XS_R[sl]], writes=[X2S_R[t]])
                if t == 0:
                    p4_load(0)
                elif t == 3:
                    p4_quant(0)
                elif t == 5:
                    p4_load(1)

            p4_quant(1)
            p4_load(2)

            for i in range(NHC):
                p4_mms(i)
                p4_post(i)
                if i + 2 < NHC:
                    p4_quant(i + 2)
                if i + 3 < NHC:
                    p4_load(i + 3)

            T2A_R = [res(f"t2a{i}") for i in range(NHC)]

            def xnt_span_res(lo, hi):
                return [XNT_R[c] for c in range(lo // XNT_STRIDE, min(ND - 1, (hi - 1) // XNT_STRIDE) + 1)]

            def t2_quant(dg, i):
                sub, sres = wb_slot(i)
                dst = XNT[:, 512 * i:512 * (i + 1)]
                wres = [T2A_R[i]] + xnt_span_res(512 * i, 512 * (i + 1))
                kb.dma("gpsimd", dsems[f"pw{i % 8}"],
                       lambda E, i=i, dg=dg, sub=sub: E.dma_start(
                           out=sub, in_=w2t_d[128 * i:128 * (i + 1), 512 * dg:512 * (dg + 1)]),
                       writes=[sres])
                kb.I("vector", lambda E, sub=sub, dst=dst: E.tensor_scalar(
                    dst, sub, stat["h2s"], None, ALU.is_ge, ALU.bypass),
                    reads=[sres], sreads=[ST_R["h2s"]], writes=wres)
                kb.I("vector", lambda E, sub=sub, dst=dst: E.scalar_tensor_tensor(
                    dst, sub, stat["nh2s"], dst, ALU.is_le, ALU.subtract),
                    reads=[sres, T2A_R[i]], sreads=[ST_R["nh2s"]], writes=wres)

            def p6_tile(t):
                sl = t % 2
                par = t % 2
                if t >= 2:
                    kb.dma("sync", dsems[f"xs{sl}"],
                           lambda E, t=t, sl=sl: E.dma_start(
                               out=XS[sl][:], in_=x2s_d[128 * t:128 * (t + 1), :]),
                           reads=list(X2S_R), writes=[XS_R[sl]])
                sc_b, sc_r = (scr, SCR_R) if par == 0 else (scr2, SCR2_R)
                rk, nk = ln_stats(acc_tile(t)[:], ACC_R[t], par, trash=(sc_b, sc_r))
                ln_apply(acc_tile(t)[:], ACC_R[t], sc_b[:], sc_r, rk, nk)
                kb.I("vector", lambda E, sl=sl, sc_b=sc_b: E.tensor_tensor(
                    XS[sl][:], XS[sl][:], sc_b[:], ALU.add),
                    reads=[XS_R[sl], sc_r], writes=[XS_R[sl]])
                kb.dma("sync", dsems["store"],
                       lambda E, t=t, sl=sl: E.dma_start(
                           out=out_d[128 * t:128 * (t + 1), :], in_=XS[sl][:]),
                       reads=[XS_R[sl]], writes=[])

            t2_quant(0, 0)
            t2_quant(0, 1)
            for dg in range(4):
                for tp in range(2):
                    for i in range(NHC):
                        if dg == 0 and tp == 0 and i + 2 < NHC:
                            t2_quant(0, i + 2)
                        for tt in range(4):
                            t_glob = 4 * tp + tt
                            kb.I("tensor", lambda E, i=i, tt=tt, tg=t_glob: E.matmul(
                                pf[tt][:, 0:512], G[:, 1024 * i + 128 * tg:1024 * i + 128 * (tg + 1)],
                                XNT[:, 512 * i:512 * (i + 1)], start=(i == 0), stop=(i == NHC - 1)),
                                reads=[G_R[i], T2A_R[i]], writes=[PF_R[tt]])
                        if tp == 1 and dg < 3:
                            t2_quant(dg + 1, i)
                    for tt in range(4):
                        t_glob = 4 * tp + tt
                        kb.I("scalar", lambda E, tt=tt, tg=t_glob, dg=dg: E.activation(
                            acc_tile(tg)[:, 512 * dg:512 * (dg + 1)], pf[tt][:, 0:512], AF.Identity,
                            bias=0.0, scale=stat["ns2"]),
                            reads=[PF_R[tt]], sreads=[ST_R["ns2"]], writes=[ACC_R[t_glob]])
                    if dg == 3 and tp == 0:
                        for sl in range(2):
                            kb.dma("sync", dsems[f"xs{sl}"],
                                   lambda E, sl=sl: E.dma_start(
                                       out=XS[sl][:], in_=x2s_d[128 * sl:128 * (sl + 1), :]),
                                   reads=list(X2S_R), writes=[XS_R[sl]])
                        for t6 in range(4):
                            p6_tile(t6)
                    if dg == 3 and tp == 1:
                        for t6 in range(4, 8):
                            p6_tile(t6)


        kb.t = Tracker("scan")
        walk()
        marks = kb.t.marks
        scan_idx = dict(kb.t.idx)
        kb.t = Tracker("emit", marks)
        walk()
        assert kb.t.idx == scan_idx, "walk passes diverged"
        t = kb.t

        n_store = t.dma_n[dsems["store"]] * 16
        t.ops["sync"].append(lambda E, h=dsems["store"].h, c=n_store: E.wait_ge(h, c))
        t.abs_ops["sync"].append(("wait", "d_store", n_store))
        deadlock_check(t.abs_ops)

        with nc.Block() as block:
            for eng in ENGINES:
                getattr(block, eng)(lambda E, ops=t.ops[eng]: [f(E) for f in ops])

        stats = {e: t.idx[e] for e in ENGINES}
        stats["marks"] = len(marks)
        stats["waits"] = t.n_waits
        build_nc.stats = stats
    return nc


def kernel(L0_attn_x, L0_attn_Wq, L0_attn_bq, L0_attn_Wk, L0_attn_bk,
           L0_attn_Wv, L0_attn_bv, L0_ffn_w1, L0_ffn_w3, L0_ffn_w2):
    x = np.ascontiguousarray(np.asarray(L0_attn_x, dtype=np.float32))
    WqT = np.ascontiguousarray(np.asarray(L0_attn_Wq, dtype=np.float32).T)
    WkT = np.ascontiguousarray(np.asarray(L0_attn_Wk, dtype=np.float32).T)
    WvT = np.ascontiguousarray(np.asarray(L0_attn_Wv, dtype=np.float32).T)
    bq = np.asarray(L0_attn_bq, dtype=np.float32)
    bk = np.asarray(L0_attn_bk, dtype=np.float32)
    bv = np.asarray(L0_attn_bv, dtype=np.float32)
    w1 = np.asarray(L0_ffn_w1, dtype=np.float32)
    w3 = np.asarray(L0_ffn_w3, dtype=np.float32)
    w2 = np.asarray(L0_ffn_w2, dtype=np.float32)

    w1tp = np.zeros((D, HIDP), np.float32); w1tp[:, :HID] = w1.T
    w3tp = np.zeros((D, HIDP), np.float32); w3tp[:, :HID] = w3.T
    w2tp = np.zeros((HIDP, D), np.float32); w2tp[:HID] = w2.T
    bqs = np.ascontiguousarray(np.concatenate(
        [bq.reshape(16, 128).T, bk.reshape(16, 128).T, bv.reshape(16, 128).T], axis=1))

    nc = build_nc()
    in_maps = []
    B, N = x.shape[0], x.shape[1]
    for core in range(8):
        b, hh = core // 2, core % 2
        s = hh * TPC
        left = x[b, (s - 128) % N:(s - 128) % N + 128]
        right = x[b, (s + TPC) % N:(s + TPC) % N + 128]
        x_ext = np.ascontiguousarray(np.concatenate([left, x[b, s:s + TPC], right], axis=0))
        in_maps.append({"x": x_ext, "wqt": WqT, "wkt": WkT, "wvt": WvT, "bqs": bqs,
                        "w1t": w1tp, "w3t": w3tp, "w2t": w2tp})
    res = run_bass_kernel_spmd(nc, in_maps, core_ids=list(range(8)))
    out = np.empty((B, N, D), np.float32)
    for core in range(8):
        b, hh = core // 2, core % 2
        out[b, hh * TPC:(hh + 1) * TPC] = res.results[core]["out"]
    return out


if __name__ == "__main__":
    nc = build_nc()
    print("instructions:", build_nc.stats)
